# version 18 (final)
# baseline (speedup 1.0000x reference)
import os
import numpy as np
KSER = int(os.environ.get("KSER", "0"))
import ml_dtypes
import concourse.bass as bass
import concourse.mybir as mybir
from concourse.bass_utils import run_bass_kernel_spmd
from contextlib import ExitStack

F32 = mybir.dt.float32
BF16 = mybir.dt.bfloat16
I32 = mybir.dt.int32
AF = mybir.ActivationFunctionType
ALU = mybir.AluOpType
AX = mybir.AxisListType

D = 2048
NT = 1024
NTILE = 8
NC = 16
CMAG = 12582912.0
LN_EPS = 1e-5
RMS_EPS = float(np.finfo(np.float32).eps)
Q_EPS = 1e-5
SC_HD = float(1.0 / np.sqrt(128.0))
DFF = 5504
NIC = 43
ENGS = ("sync", "vector", "scalar", "tensor", "gpsimd")


def tiles512(O, C):
    res = []
    for o0 in range(0, O, 128):
        c0 = 0
        while c0 < C:
            cw = min(512, C - c0)
            res.append((o0, c0, cw))
            c0 += cw
    return res


class Tok:
    __slots__ = ("ctr", "val")

    def __init__(self, ctr, val):
        self.ctr = ctr
        self.val = val


class Em:
    def __init__(self, mode, ctrs, engine=None, eng_obj=None, tokens=None):
        self.mode = mode
        self.ctrs = ctrs
        self.vals = {k: c.v for k, c in ctrs.items()}
        self.engine = engine
        self.eng_obj = eng_obj
        self.tokens = tokens if tokens is not None else []
        self.ti = 0
        self.wmax = {}
        self.last = None
        self.fence = None

    def do(self, eng, fn, inc=None, wait=(), manual=False, fence=False):
        if inc is not None and not isinstance(inc, tuple):
            inc = (inc, 1)
        if self.mode == "count":
            tok = None
            if inc is not None:
                self.vals[inc[0]] += inc[1]
                tok = Tok(inc[0], self.vals[inc[0]])
            self.tokens.append(tok)
            self.ti += 1
            return tok
        tok = self.tokens[self.ti]
        self.ti += 1
        if KSER:
            wait = list(wait) + ([self.last] if self.last is not None else [])
            self.last = tok if tok is not None else self.last
        if eng != self.engine:
            return tok
        e = self.eng_obj
        for w in wait:
            if w is None:
                continue
            if self.wmax.get(w.ctr, -1) >= w.val:
                continue
            e.wait_ge(self.ctrs[w.ctr].h, w.val)
            self.wmax[w.ctr] = w.val
        ins = fn(e)
        if inc is not None and not manual:
            if ins is None:
                e.sem_inc(self.ctrs[inc[0]].h, inc[1])
            elif fence and eng in ("vector", "scalar"):
                e.drain()
                e.drain().then_inc(self.ctrs[inc[0]].h, inc[1])
            else:
                ins.then_inc(self.ctrs[inc[0]].h, inc[1])
        return tok


def build():
    nc = bass.Bass("TRN2", target_bir_lowering=False)

    x0 = nc.dram_tensor("x0", [2048, D], F32, kind="ExternalInput").ap()
    maskin = nc.dram_tensor("maskin", [128, 16 * NT], BF16, kind="ExternalInput").ap()
    identin = nc.dram_tensor("identin", [128, 128], BF16, kind="ExternalInput").ap()
    identfin = nc.dram_tensor("identfin", [128, 128], F32, kind="ExternalInput").ap()
    noncein = nc.dram_tensor("noncein", [1, 32], I32, kind="ExternalInput").ap()
    out = nc.dram_tensor("out", [NT, D], F32, kind="ExternalOutput").ap()
    w_a1 = nc.dram_tensor("w_a1", [2048, 2048], F32, kind="ExternalInput").ap()
    w_a2 = nc.dram_tensor("w_a2", [512, 2048], F32, kind="ExternalInput").ap()
    w_a3 = nc.dram_tensor("w_a3", [512, 2048], F32, kind="ExternalInput").ap()
    w_g = nc.dram_tensor("w_g", [6144, 2048], F32, kind="ExternalInput").ap()
    w_t1 = nc.dram_tensor("w_t1", [5504, 2048], F32, kind="ExternalInput").ap()
    w_t2 = nc.dram_tensor("w_t2", [2048, 5504], F32, kind="ExternalInput").ap()

    SH = dict(kind="Internal", addr_space="Shared")
    wqT = nc.dram_tensor("wqT", [2048, 2048], BF16, **SH).ap()
    wkT = nc.dram_tensor("wkT", [2048, 512], BF16, **SH).ap()
    wvT = nc.dram_tensor("wvT", [2048, 512], BF16, **SH).ap()
    woT = nc.dram_tensor("woT", [2048, 2048], BF16, **SH).ap()
    wqkvT = nc.dram_tensor("wqkvT", [2048, 6144], BF16, **SH).ap()
    wprojT = nc.dram_tensor("wprojT", [2048, 2048], BF16, **SH).ap()
    wgT = nc.dram_tensor("wgT", [2048, DFF], BF16, **SH).ap()
    wuT = nc.dram_tensor("wuT", [2048, DFF], BF16, **SH).ap()
    wdT = nc.dram_tensor("wdT", [DFF, 2048], BF16, **SH).ap()
    kT_sh = nc.dram_tensor("kT_sh", [2, 2048, NT], BF16, **SH).ap()
    v_sh = nc.dram_tensor("v_sh", [2, NT, 2048], BF16, **SH).ap()
    flag_sh = nc.dram_tensor("flag_sh", [2, 4, 8], I32, **SH).ap()
    x_hbm = nc.dram_tensor("x_hbm", [NT, D], F32, kind="Internal").ap()
    ppdump = nc.dram_tensor("ppdump", [128, 128], BF16, kind="Internal").ap()
    KDBG = int(os.environ.get("KDBG", "0"))
    KDBG2 = int(os.environ.get("KDBG2", "0"))
    if KDBG:
        d_wq = nc.dram_tensor("d_wq", [2048, 2048], BF16, kind="ExternalOutput").ap()
        d_wg = nc.dram_tensor("d_wg", [2048, DFF], BF16, kind="ExternalOutput").ap()
        d_wd = nc.dram_tensor("d_wd", [DFF, 2048], BF16, kind="ExternalOutput").ap()
        d_wu = nc.dram_tensor("d_wu", [2048, DFF], BF16, kind="ExternalOutput").ap()
        d_x = [nc.dram_tensor(f"d_x{i}", [NT, D], F32, kind="ExternalOutput").ap()
               for i in (1, 2, 3)]
        d_xqt = nc.dram_tensor("d_xqt", [128, NC * NT], BF16,
                               kind="ExternalOutput").ap()
        d_lst = nc.dram_tensor("d_lst", [8 * 128, 16], F32,
                               kind="ExternalOutput").ap()
    if KDBG2:
        d_f1 = nc.dram_tensor("d_f1", [128, NC * NT], BF16, kind="ExternalOutput").ap()
        d_f2 = nc.dram_tensor("d_f2", [128, NIC * 512], BF16, kind="ExternalOutput").ap()
        d_f3 = nc.dram_tensor("d_f3", [128, 1024], F32, kind="ExternalOutput").ap()
        d_f4 = nc.dram_tensor("d_f4", [8 * 128, 16], F32, kind="ExternalOutput").ap()
        d_a1 = nc.dram_tensor("d_a1", [128, NC * NT], BF16, kind="ExternalOutput").ap()
        d_a2 = nc.dram_tensor("d_a2", [128, 4 * 2048], BF16, kind="ExternalOutput").ap()
        d_a3 = nc.dram_tensor("d_a3", [128, 16 * 512], BF16, kind="ExternalOutput").ap()
        d_a4 = nc.dram_tensor("d_a4", [128, 4 * NT], BF16, kind="ExternalOutput").ap()
        d_a5 = nc.dram_tensor("d_a5", [128, NC * NT], BF16, kind="ExternalOutput").ap()

    ctx = ExitStack()
    with ctx:
        E = ctx.enter_context

        xqT = E(nc.sbuf_tensor("xqT", [128, NC * NT], BF16))
        xt = E(nc.sbuf_tensor("xt", [128, 3 * D], F32))
        xnf = E(nc.sbuf_tensor("xnf", [128, D], F32))
        xqtok = E(nc.sbuf_tensor("xqtok", [128, 2 * D], BF16))
        stats = E(nc.sbuf_tensor("stats", [128, 32], F32))
        ones_b = E(nc.sbuf_tensor("ones_b", [128, 128], BF16))
        ones_f = E(nc.sbuf_tensor("ones_f", [128, 128], F32))
        ident = E(nc.sbuf_tensor("ident", [128, 128], BF16))
        identf = E(nc.sbuf_tensor("identf", [128, 128], F32))
        snonce = E(nc.sbuf_tensor("snonce", [1, 32], I32))
        sflag = E(nc.sbuf_tensor("sflag", [1, 8], I32))
        rowb0 = E(nc.sbuf_tensor("rowb0", [1, NT], F32))
        rowb1 = E(nc.sbuf_tensor("rowb1", [1, NT], F32))
        rowb2 = E(nc.sbuf_tensor("rowb2", [1, NT], F32))
        rowb3 = E(nc.sbuf_tensor("rowb3", [1, NT], F32))

        class Ctr:
            def __init__(self, name):
                self.h = E(nc.semaphore(name))
                self.v = 0
        ctrs = {n: Ctr(n) for n in
                ("ld", "gd", "vc", "sc", "tc", "gc", "ph", "pl", "st")}
        ld, gd, vc, sc, tc, gc, ph, pl, st = (ctrs[n] for n in
                ("ld", "gd", "vc", "sc", "tc", "gc", "ph", "pl", "st"))

        block = E(nc.Block())
        nbar = [0]

        def barrier():
            nbar[0] += 1
            tgt = 5 * nbar[0]
            for name in ENGS:
                def f(eng):
                    eng.sem_inc(ph.h, 1)
                    eng.wait_ge(ph.h, tgt)
                getattr(block, name)(f)
            ph.v = tgt

        def vfence(e):
            if hasattr(e, "tensor_copy"):
                return e.tensor_copy(stats[:, 31:32], stats[:, 30:31])
            return e.copy(stats[:, 31:32], stats[:, 30:31])

        def run_phase(phase_fn, *args):
            emc = Em("count", ctrs)
            phase_fn(emc, *args)
            finals = dict(emc.vals)
            for name in ENGS:
                def f(eng, name=name):
                    eme = Em("emit", ctrs, engine=name, eng_obj=eng,
                             tokens=emc.tokens)
                    eme.fence = vfence
                    phase_fn(eme, *args)
                getattr(block, name)(f)
            for k, v in finals.items():
                ctrs[k].v = v

        regs = {}

        def get_reg(g, name):
            if name not in regs:
                regs[name] = g.alloc_register(name)
            return regs[name]

        def emit_poll(g, par, epoch):
            rl = get_reg(g, "rl")
            rf = get_reg(g, "rf")
            rn = get_reg(g, "rn")
            rt = get_reg(g, "rt")
            for _ in range(3):
                g.reg_mov(rl, 1)
                with g.While(rl):
                    g.reg_add(rt, rt, 16)
                    g.dma_start(out=sflag[:], in_=flag_sh[1 - par, epoch]
                                ).then_inc(pl.h, 16)
                    g.wait_ge(pl.h, rt)
                    g.reg_load(rf, sflag[0:1, 0:1])
                    g.reg_load(rn, snonce[0:1, 8 * epoch:8 * epoch + 1])
                    g.reg_alu(rl, rf, rn, ALU.not_equal)

        PS = {}

        def dbg_copy(dst, src_, nbytesper=16):
            def f(g):
                g.dma_start(out=dst, in_=src_).then_inc(gd.h, 16)
                g.wait_ge(gd.h, gd.v + 16)
            block.gpsimd(f)
            gd.v += 16
            barrier()

        def p0_sync(sy):
            sy.dma_start(out=ident[:], in_=identin).then_inc(ld.h, 16)
            sy.dma_start(out=identf[:], in_=identfin).then_inc(ld.h, 16)
            sy.dma_start(out=snonce[:], in_=noncein).then_inc(ld.h, 16)
        block.sync(p0_sync)
        ld.v += 48

        def p0_vec(v):
            v.wait_ge(ld.h, ld.v)
            v.memset(ones_b[:], 1.0)
            v.memset(ones_f[:], 1.0)
        block.vector(p0_vec)

        def p0_gp(g):
            rt = get_reg(g, "rt")
            g.reg_mov(rt, 0)
        block.gpsimd(p0_gp)
        barrier()

        def jobs_of(par):
            if par == 0:
                plain = [(w_a1, wqT, 2048, 2048), (w_a2, wkT, 512, 2048),
                         (w_a3, wvT, 512, 2048)]
                glob = (w_g, wqkvT, 6144, 2048)
                tern = [(w_t1, wgT, DFF, 2048)]
            else:
                plain = [(w_a1, woT, 2048, 2048)]
                glob = (w_g, wprojT, 2048, 2048)
                tern = [(w_t1, wuT, DFF, 2048), (w_t2, wdT, 2048, DFF)]
            jobs = []
            for (s, d, O, C) in plain:
                for (o0, c0, cw) in tiles512(O, C):
                    jobs.append(("p", s, d, o0, c0, cw))
            for (s, d, O, C) in tern:
                for (o0, c0, cw) in tiles512(O, C):
                    jobs.append(("t", s, d, o0, c0, cw))
            gs, gdst, gO, gC = glob
            gtiles = tiles512(gO, gC)
            for (o0, c0, cw) in gtiles:
                jobs.append(("g", gs, gdst, o0, c0, cw))
            return jobs, glob, gtiles

        jobsP, globP, gtilesP = {}, {}, {}
        for par in (0, 1):
            jobsP[par], globP[par], gtilesP[par] = jobs_of(par)
        if int(os.environ.get("KSKIP", "0")):
            for par in (0, 1):
                jobsP[par] = jobsP[par][:2]
                gtilesP[par] = gtilesP[par][:2]
        NPA = max(len(gtilesP[0]), len(gtilesP[1]))
        NPP = max(len(jobsP[0]), len(jobsP[1]))

        with ExitStack() as pctx:
            PE_ = pctx.enter_context
            ppw = PE_(nc.sbuf_tensor("ppw", [128, 2 * 512], F32))
            ppq = PE_(nc.sbuf_tensor("ppq", [128, 512], F32))
            ppb = PE_(nc.sbuf_tensor("ppb", [128, 2 * 512], BF16))
            pps = PE_(nc.sbuf_tensor("pps", [128, 2 * 512], BF16))
            ppstat = PE_(nc.sbuf_tensor("ppstat", [128, 24], F32))
            pprow = PE_(nc.sbuf_tensor("pprow", [1, 8], F32))
            ppsA = PE_(nc.psum_tensor("ppsA", [128, 512], BF16))
            ppsB = PE_(nc.psum_tensor("ppsB", [128, 512], BF16))
            ppsS = PE_(nc.psum_tensor("ppsS", [1, 8], F32))
            ppsBC = PE_(nc.psum_tensor("ppsBC", [128, 8], F32))

            L0, V0, T0, S0, G0 = ld.v, vc.v, tc.v, sc.v, gd.v

            def p1_sync(sy):
                pid = sy.partition_id()
                for par in (0, 1):
                    cm = sy.If((pid % 2) == 0) if par == 0 else sy.Else()
                    with cm:
                        gs = globP[par][0]
                        gt = gtilesP[par]
                        for i in range(NPA):
                            if i < len(gt):
                                o0, c0, cw = gt[i]
                                if i >= 2:
                                    sy.wait_ge(vc.h, V0 + i - 1)
                                sy.dma_start(
                                    out=ppw[:, 512 * (i % 2):512 * (i % 2) + cw],
                                    in_=gs[o0:o0 + 128, c0:c0 + cw]
                                ).then_inc(ld.h, 16)
                            else:
                                sy.sem_inc(ld.h, 16)
                        jb = jobsP[par]
                        for i in range(NPP):
                            if i < len(jb):
                                k, s, d, o0, c0, cw = jb[i]
                                if i >= 2:
                                    sy.wait_ge(vc.h, V0 + NPA + 3 + i - 1)
                                sy.dma_start(
                                    out=ppw[:, 512 * (i % 2):512 * (i % 2) + cw],
                                    in_=s[o0:o0 + 128, c0:c0 + cw]
                                ).then_inc(ld.h, 16)
                            else:
                                sy.sem_inc(ld.h, 16)
            block.sync(p1_sync)
            ld.v += 16 * (NPA + NPP)

            def p1_vec(v):
                pid = v.partition_id()
                for par in (0, 1):
                    cm = v.If((pid % 2) == 0) if par == 0 else v.Else()
                    with cm:
                        gt = gtilesP[par]
                        dm = lambda: v.drain()
                        v.memset(ppstat[:, 0:4], 0.0)
                        for i in range(NPA):
                            if i < len(gt):
                                cw = gt[i][2]
                                v.wait_ge(ld.h, L0 + 16 * (i + 1))
                                v.reduce_sum(ppstat[:, 4:5],
                                             ppw[:, 512 * (i % 2):512 * (i % 2) + cw],
                                             axis=AX.X, apply_absolute_value=True)
                                dm()
                                dm()
                                v.tensor_scalar(ppstat[:, i % 4:i % 4 + 1],
                                                ppstat[:, 4:5],
                                                ppstat[:, i % 4:i % 4 + 1],
                                                0.0, ALU.add,
                                                ALU.add).then_inc(vc.h, 1)
                            else:
                                v.sem_inc(vc.h, 1)
                        v.tensor_scalar(ppstat[:, 5:6], ppstat[:, 0:1],
                                        ppstat[:, 1:2], 0.0, ALU.add, ALU.add)
                        v.tensor_scalar(ppstat[:, 6:7], ppstat[:, 2:3],
                                        ppstat[:, 3:4], 0.0, ALU.add, ALU.add)
                        dm()
                        dm()
                        v.tensor_scalar(ppstat[:, 0:1], ppstat[:, 5:6],
                                        ppstat[:, 6:7], 0.0, ALU.add, ALU.add)
                        dm()
                        dm().then_inc(vc.h, 1)
                        gO = globP[par][2]
                        v.wait_ge(tc.h, T0 + 1)
                        v.tensor_scalar(pprow[0:1, 0:1], ppsS[0:1, 0:1],
                                        1.0 / float(gO * 2048), Q_EPS,
                                        ALU.mult, ALU.max)
                        dm()
                        dm()
                        v.reciprocal(pprow[0:1, 1:2], pprow[0:1, 0:1])
                        dm()
                        dm().then_inc(vc.h, 1)
                        v.wait_ge(tc.h, T0 + 2)
                        v.tensor_copy(ppstat[:, 2:4], ppsBC[:, 0:2]).then_inc(vc.h, 1)
                        jb = jobsP[par]
                        for i in range(NPP):
                            if i < len(jb):
                                k, s, d, o0, c0, cw = jb[i]
                                wsl = ppw[:, 512 * (i % 2):512 * (i % 2) + cw]
                                bsl = ppb[:, 512 * (i % 2):512 * (i % 2) + cw]
                                v.wait_ge(ld.h, L0 + 16 * (NPA + i + 1))
                                if i >= 2:
                                    v.wait_ge(tc.h, T0 + 2 + 4 * (i - 1))
                                if k == "p":
                                    v.tensor_copy(bsl, wsl).then_inc(vc.h, 1)
                                elif k == "t":
                                    nseg = cw // 128
                                    segs = [wsl[:, 128 * sg:128 * sg + 128]
                                            for sg in range(nseg)]
                                    for sg in range(nseg):
                                        v.scalar_tensor_tensor(
                                            ppq[:, 128 * sg:128 * sg + 128],
                                            segs[sg], -1.0, segs[sg],
                                            ALU.mult, ALU.max)
                                    v.drain()
                                    v.drain()
                                    for hw in (64, 32, 16, 8, 4, 2, 1):
                                        for sg in range(nseg):
                                            b0 = 128 * sg
                                            v.tensor_tensor(
                                                ppq[:, b0:b0 + hw],
                                                ppq[:, b0:b0 + hw],
                                                ppq[:, b0 + hw:b0 + 2 * hw],
                                                ALU.add)
                                        v.drain()
                                        v.drain()
                                    for sg in range(nseg):
                                        v.tensor_scalar(ppstat[:, 8 + sg:9 + sg],
                                                        ppq[:, 128 * sg:
                                                            128 * sg + 1],
                                                        1.0 / 128.0, Q_EPS,
                                                        ALU.mult, ALU.add)
                                    v.drain()
                                    v.drain()
                                    for sg in range(nseg):
                                        v.reciprocal(ppstat[:, 12 + sg:13 + sg],
                                                     ppstat[:, 8 + sg:9 + sg])
                                    v.drain()
                                    v.drain()
                                    for sg in range(nseg):
                                        v.tensor_tensor(
                                            ppstat[:, 16 + sg:17 + sg],
                                            ppstat[:, 8 + sg:9 + sg],
                                            ppstat[:, 12 + sg:13 + sg],
                                            ALU.mult)
                                    v.drain()
                                    v.drain()
                                    for sg in range(nseg):
                                        v.tensor_scalar(
                                            ppstat[:, 20 + sg:21 + sg],
                                            ppstat[:, 16 + sg:17 + sg], -1.0,
                                            2.0, ALU.mult, ALU.add)
                                    v.drain()
                                    v.drain()
                                    for sg in range(nseg):
                                        v.tensor_tensor(
                                            ppstat[:, 16 + sg:17 + sg],
                                            ppstat[:, 20 + sg:21 + sg],
                                            ppstat[:, 12 + sg:13 + sg],
                                            ALU.mult)
                                    v.drain()
                                    v.drain()
                                    for sg in range(nseg):
                                        v.tensor_scalar(
                                            ppq[:, 128 * sg:128 * sg + 128],
                                            segs[sg],
                                            ppstat[:, 16 + sg:17 + sg], CMAG,
                                            ALU.mult, ALU.add)
                                    v.tensor_scalar(ppq[:, 0:cw], ppq[:, 0:cw],
                                                    CMAG + 1.0, CMAG - 1.0,
                                                    ALU.min, ALU.max)
                                    for sg in range(nseg):
                                        ins = v.tensor_scalar(
                                            bsl[:, 128 * sg:128 * sg + 128],
                                            ppq[:, 128 * sg:128 * sg + 128],
                                            -CMAG, ppstat[:, 8 + sg:9 + sg],
                                            ALU.add, ALU.mult)
                                    ins.then_inc(vc.h, 1)
                                else:
                                    v.tensor_scalar(ppq[:, 0:cw], wsl,
                                                    ppstat[:, 3:4], CMAG,
                                                    ALU.mult, ALU.add)
                                    v.tensor_scalar(ppq[:, 0:cw], ppq[:, 0:cw],
                                                    CMAG + 1.0, CMAG - 1.0,
                                                    ALU.min, ALU.max)
                                    v.tensor_scalar(bsl, ppq[:, 0:cw], -CMAG,
                                                    ppstat[:, 2:3], ALU.add,
                                                    ALU.mult).then_inc(vc.h, 1)
                            else:
                                v.sem_inc(vc.h, 1)
            block.vector(p1_vec)
            vc.v += NPA + 3 + NPP

            def p1_tensor(t):
                pid = t.partition_id()
                for par in (0, 1):
                    cm = t.If((pid % 2) == 0) if par == 0 else t.Else()
                    with cm:
                        t.wait_ge(vc.h, V0 + NPA + 1)
                        t.matmul(ppsS[0:1, 0:1], ones_f[:, 0:1], ppstat[:, 0:1],
                                 start=True, stop=True).then_inc(tc.h, 1)
                        t.wait_ge(vc.h, V0 + NPA + 2)
                        t.matmul(ppsBC[:, 0:2], ones_f[0:1, :], pprow[0:1, 0:2],
                                 start=True, stop=True).then_inc(tc.h, 1)
                        jb = jobsP[par]
                        for i in range(NPP):
                            if i < len(jb):
                                k, s, d, o0, c0, cw = jb[i]
                                bsl = ppb[:, 512 * (i % 2):512 * (i % 2) + 512]
                                ps = ppsA if i % 2 == 0 else ppsB
                                t.wait_ge(vc.h, V0 + NPA + 3 + i + 1)
                                if i >= 2:
                                    t.wait_ge(sc.h, S0 + i - 1)
                                nseg = cw // 128
                                for kk in range(4):
                                    sk = min(kk, nseg - 1)
                                    t.transpose(ps[:, 128 * kk:128 * kk + 128],
                                                bsl[:, 128 * sk:128 * sk + 128],
                                                ident[:]).then_inc(tc.h, 1)
                            else:
                                t.sem_inc(tc.h, 4)
            block.tensor(p1_tensor)
            tc.v += 2 + 4 * NPP

            def p1_scalar(s):
                pid = s.partition_id()
                for par in (0, 1):
                    cm = s.If((pid % 2) == 0) if par == 0 else s.Else()
                    with cm:
                        jb = jobsP[par]
                        for i in range(NPP):
                            if i < len(jb):
                                ps = ppsA if i % 2 == 0 else ppsB
                                ssl = pps[:, 512 * (i % 2):512 * (i % 2) + 512]
                                s.wait_ge(tc.h, T0 + 2 + 4 * (i + 1))
                                if i >= 2:
                                    s.wait_ge(gd.h, G0 + 64 * (i - 1))
                                s.copy(ssl, ps[:]).then_inc(sc.h, 1)
                            else:
                                s.sem_inc(sc.h, 1)
            block.scalar(p1_scalar)
            sc.v += NPP

            def p1_gpsimd(g):
                pid = g.partition_id()
                for par in (0, 1):
                    cm = g.If((pid % 2) == 0) if par == 0 else g.Else()
                    with cm:
                        jb = jobsP[par]
                        for i in range(NPP):
                            if i < len(jb):
                                k, s_, d, o0, c0, cw = jb[i]
                                ssl = pps[:, 512 * (i % 2):512 * (i % 2) + 512]
                                g.wait_ge(sc.h, S0 + i + 1)
                                nseg = cw // 128
                                for kk in range(4):
                                    if kk < nseg:
                                        g.dma_start(
                                            out=d[c0 + 128 * kk:
                                                  c0 + 128 * kk + 128,
                                                  o0:o0 + 128],
                                            in_=ssl[:, 128 * kk:128 * kk + 128]
                                        ).then_inc(gd.h, 16)
                                    else:
                                        g.dma_start(out=ppdump,
                                                    in_=ssl[:, 0:128]
                                                    ).then_inc(gd.h, 16)
                            else:
                                for kk in range(4):
                                    g.dma_start(out=ppdump, in_=pps[:, 0:128]
                                                ).then_inc(gd.h, 16)
                        g.wait_ge(gd.h, G0 + 64 * NPP)
                        g.dma_start(out=flag_sh[par, 0],
                                    in_=snonce[0:1, 0:8]).then_inc(gd.h, 16)
                        g.wait_ge(gd.h, G0 + 64 * NPP + 16)
                        emit_poll(g, par, 0)
            block.gpsimd(p1_gpsimd)
            gd.v += 64 * NPP + 16
            barrier()

        def phase_lnq(em, src_ap, row0, quantize, ntiles=NTILE, xqt_off=0):
            lnq_psum = PS["lnq"]
            sq_t, rs_t, nm_t, ev_t, vb_t, nb_t, xq_t, tr_t = ({} for _ in range(8))
            for t in range(ntiles):
                xsl = xt[:, D * (t % 3):D * (t % 3) + D]
                nsl = xnf[:, 0:D]
                qsl = xqtok[:, D * (t % 2):D * (t % 2) + D]
                w_load = [nm_t[t - 3]] if t >= 3 else []
                tok_ld = em.do(
                    "sync",
                    lambda sy, t=t, xsl=xsl: sy.dma_start(
                        out=xsl,
                        in_=src_ap[row0 + 128 * t:row0 + 128 * t + 128, :]),
                    inc=("ld", 16), wait=w_load)
                wsq = [tok_ld]
                if t >= 1:
                    wsq.append(xq_t[t - 1] if quantize else nb_t[t - 1])
                sq_t[t] = em.do(
                    "scalar",
                    lambda s, nsl=nsl, xsl=xsl: s.activation(
                        nsl, xsl, AF.Square),
                    inc="sc", wait=wsq)

                def fvb(v, xsl=xsl, nsl=nsl):
                    dm = lambda: v.drain()
                    v.reduce_sum(stats[:, 0:1], xsl, axis=AX.X)
                    v.reduce_sum(stats[:, 1:2], nsl, axis=AX.X)
                    v.tensor_scalar(stats[:, 2:3], stats[:, 0:1], 1.0 / D, 0.0,
                                    ALU.mult, ALU.add)
                    v.tensor_scalar(stats[:, 4:5], stats[:, 1:2], 1.0 / D,
                                    LN_EPS, ALU.mult, ALU.add)
                    dm()
                    v.tensor_scalar(stats[:, 3:4], stats[:, 2:3],
                                    stats[:, 2:3], 0.0, ALU.mult, ALU.add)
                    dm()
                    dm()
                    return v.tensor_scalar(stats[:, 6:7], stats[:, 4:5],
                                           stats[:, 3:4], 0.0, ALU.subtract,
                                           ALU.add)
                vb_t[t] = em.do("vector", fvb, inc="vc",
                                wait=[tok_ld, sq_t[t]], fence=True)
                rs_t[t] = em.do(
                    "scalar",
                    lambda s: s.activation(stats[:, 14:15], stats[:, 6:7],
                                           AF.Sqrt),
                    inc="sc", wait=[vb_t[t]])

                def fnb(v):
                    dm = lambda: v.drain()
                    v.reciprocal(stats[:, 7:8], stats[:, 14:15])
                    dm()
                    dm()
                    v.tensor_scalar(stats[:, 8:9], stats[:, 2:3],
                                    -1.0, stats[:, 7:8], ALU.mult,
                                    ALU.mult)
                    dm()
                    return dm()
                nb_t[t] = em.do("vector", fnb, inc="vc", wait=[rs_t[t]])
                wn = [nb_t[t]]
                if quantize:
                    dst = nsl
                else:
                    dst = qsl
                    if t >= 2:
                        wn.append(tr_t[t - 2])
                nm_t[t] = em.do(
                    "vector",
                    lambda v, dst=dst, xsl=xsl: v.tensor_scalar(
                        dst, xsl, stats[:, 7:8], stats[:, 8:9],
                        ALU.mult, ALU.add),
                    inc="vc", wait=wn, fence=(not quantize))
                if quantize:
                    def fq(v, nsl=nsl, qsl=qsl):
                        dm = lambda: v.drain()
                        v.reduce_max(stats[:, 9:10], nsl, axis=AX.X,
                                     apply_absolute_value=True)
                        dm()
                        dm()
                        v.tensor_scalar(stats[:, 10:11], stats[:, 9:10], Q_EPS,
                                        0.0, ALU.max, ALU.add)
                        dm()
                        dm()
                        v.reciprocal(stats[:, 11:12], stats[:, 10:11])
                        v.tensor_scalar(stats[:, 13:14], stats[:, 10:11],
                                        1.0 / 127.0, 0.0, ALU.mult, ALU.add)
                        dm()
                        v.tensor_scalar(stats[:, 12:13], stats[:, 11:12], 127.0,
                                        0.0, ALU.mult, ALU.add)
                        dm()
                        dm()
                        v.tensor_scalar(nsl, nsl, stats[:, 12:13], CMAG,
                                        ALU.mult, ALU.add)
                        return v.tensor_scalar(qsl, nsl, -CMAG, stats[:, 13:14],
                                               ALU.add, ALU.mult)
                    wq = [nm_t[t]] + ([tr_t[t - 2]] if t >= 2 else [])
                    xq_t[t] = em.do("vector", fq, inc="vc", wait=wq,
                                    fence=True)

                if KDBG and PS.get("lndump"):
                    em.do("gpsimd",
                          lambda g, t=t: g.dma_start(
                              out=d_lst[128 * t:128 * t + 128, :],
                              in_=stats[:, 0:16]),
                          inc=("gd", 16),
                          wait=[nm_t[t], xq_t.get(t), nb_t[t]])

                def ftr(t_, qsl=qsl):
                    for c in range(NC):
                        ins = t_.transpose(lnq_psum[:, 128 * c:128 * c + 128],
                                           qsl[:, 128 * c:128 * c + 128],
                                           ident[:])
                    return ins
                wt = [xq_t[t] if quantize else nm_t[t]]
                if t >= 1:
                    wt.append(ev_t[t - 1])
                tr_t[t] = em.do("tensor", ftr, inc="tc", wait=wt)

                def fev(s, t=t):
                    for c in range(NC):
                        ins = s.copy(xqT[:, NT * c + 128 * (t + xqt_off):
                                         NT * c + 128 * (t + xqt_off) + 128],
                                     lnq_psum[:, 128 * c:128 * c + 128])
                    return ins
                ev_t[t] = em.do("scalar", fev, inc="sc", wait=[tr_t[t]])

        def phase_proj_fm(em, wT_ap, o_lo, o_hi, dst_slice, src_col0=0,
                          nfree=NT):
            proj_psum = PS["proj"]
            wslice = PS["wsl"]
            nsl = (o_hi - o_lo) // 512
            nq = nfree // 512
            mm_t, ev_t = {}, {}
            cnt = 0
            for j in range(nsl):
                wld = [mm_t[(j - 1, 3, nq - 1)]] if j >= 1 else []
                tok_w = em.do(
                    "sync",
                    lambda sy, j=j: sy.dma_start(
                        out=wslice[:].rearrange("p (k o) -> p k o", k=NC),
                        in_=wT_ap[:, o_lo + 512 * j:o_lo + 512 * j + 512]
                        .rearrange("(k p) o -> p k o", p=128)),
                    inc=("ld", 16), wait=wld)
                for m in range(4):
                    for q in range(nq):
                        cnt += 1
                        ps = proj_psum[:, 512 * (cnt % 2):512 * (cnt % 2) + 512]

                        def fmm(t_, m=m, q=q, ps=ps):
                            for c in range(NC):
                                ins = t_.matmul(
                                    ps,
                                    wslice[:, 512 * c + 128 * m:
                                           512 * c + 128 * m + 128],
                                    xqT[:, NT * c + src_col0 + 512 * q:
                                        NT * c + src_col0 + 512 * q + 512],
                                    start=(c == 0), stop=(c == NC - 1))
                            return ins
                        wmm = [tok_w]
                        if cnt > 2:
                            wmm.append(ev_t[cnt - 2])
                        mm_t[(j, m, q)] = em.do("tensor", fmm, inc="tc",
                                                wait=wmm)
                        ev_t[cnt] = em.do(
                            "scalar",
                            lambda s, j=j, m=m, q=q, ps=ps: s.copy(
                                dst_slice(4 * j + m, q), ps),
                            inc="sc", wait=[mm_t[(j, m, q)]])

        def phase_proj_tm(em, wT_ap, o_lo, o_hi, dst_slice, src_col0=0,
                          tlist=tuple(range(NTILE))):
            proj_psum = PS["proj"]
            wslice = PS["wsl"]
            nsl = (o_hi - o_lo) // 512
            tl = list(tlist)
            mm_t, ev_t = {}, {}
            cnt = 0
            for j in range(nsl):
                wld = [mm_t[(j - 1, tl[-1])]] if j >= 1 else []
                tok_w = em.do(
                    "sync",
                    lambda sy, j=j: sy.dma_start(
                        out=wslice[:].rearrange("p (k o) -> p k o", k=NC),
                        in_=wT_ap[:, o_lo + 512 * j:o_lo + 512 * j + 512]
                        .rearrange("(k p) o -> p k o", p=128)),
                    inc=("ld", 16), wait=wld)
                for t in tl:
                    cnt += 1
                    ps = proj_psum[:, 512 * (cnt % 2):512 * (cnt % 2) + 512]

                    def fmm(t_, t=t, ps=ps):
                        for c in range(NC):
                            ins = t_.matmul(
                                ps,
                                xqT[:, NT * c + src_col0 + 128 * t:
                                    NT * c + src_col0 + 128 * t + 128],
                                wslice[:, 512 * c:512 * c + 512],
                                start=(c == 0), stop=(c == NC - 1))
                        return ins
                    wmm = [tok_w]
                    if cnt > 2:
                        wmm.append(ev_t[cnt - 2])
                    mm_t[(j, t)] = em.do("tensor", fmm, inc="tc", wait=wmm)
                    ev_t[cnt] = em.do(
                        "scalar",
                        lambda s, j=j, t=t, ps=ps: s.copy(dst_slice(t, j), ps),
                        inc="sc", wait=[mm_t[(j, t)]])

        def phase_out_proj(em, wT_ap, yT_buf, resid_src, resid_dst, store_ctr):
            proj_psum = PS["proj"]
            wfull = PS["wfull"]
            tok_w = em.do(
                "sync",
                lambda sy: sy.dma_start(
                    out=wfull[:].rearrange("p (k o) -> p k o", k=NC),
                    in_=wT_ap[:].rearrange("(k p) o -> p k o", p=128)),
                inc=("ld", 16))
            mm_t, add_t, st_t, ldx_t = {}, {}, {}, {}
            cnt = 0
            for t in range(NTILE):
                wlx = [add_t[(t - 2, 3)]] if t >= 2 else []
                ldx_t[t] = em.do(
                    "sync",
                    lambda sy, t=t: sy.dma_start(
                        out=xt[:, D * (t % 2):D * (t % 2) + D],
                        in_=resid_src[128 * t:128 * t + 128, :]),
                    inc=("ld", 16), wait=wlx)
                for jj in range(4):
                    cnt += 1
                    ps = proj_psum[:, 512 * (cnt % 2):512 * (cnt % 2) + 512]

                    def fmm(t_, t=t, jj=jj, ps=ps):
                        for k in range(NC):
                            ins = t_.matmul(
                                ps,
                                yT_buf[:, NT * k + 128 * t:
                                       NT * k + 128 * t + 128],
                                wfull[:, 2048 * k + 512 * jj:
                                      2048 * k + 512 * jj + 512],
                                start=(k == 0), stop=(k == NC - 1))
                        return ins
                    wmm = [tok_w]
                    if cnt > 2:
                        prev = (t, jj - 2) if jj >= 2 else (t - 1, jj + 2)
                        wmm.append(add_t[prev])
                    mm_t[(t, jj)] = em.do("tensor", fmm, inc="tc", wait=wmm)
                    wadd = [mm_t[(t, jj)], ldx_t[t]]
                    if t >= 1:
                        wadd.append(st_t[t - 1])
                    add_t[(t, jj)] = em.do(
                        "vector",
                        lambda v, t=t, jj=jj, ps=ps: v.tensor_tensor(
                            xt[:, 2 * D + 512 * jj:2 * D + 512 * jj + 512],
                            ps, xt[:, D * (t % 2) + 512 * jj:
                                   D * (t % 2) + 512 * jj + 512], ALU.add),
                        inc="vc", wait=wadd, fence=True)
                st_t[t] = em.do(
                    "gpsimd",
                    lambda g, t=t: g.dma_start(
                        out=resid_dst[128 * t:128 * t + 128, :],
                        in_=xt[:, 2 * D:2 * D + D]),
                    inc=(store_ctr, 16),
                    wait=[add_t[(t, jj)] for jj in range(4)])

        with ExitStack() as actx:
            AE = actx.enter_context
            QT = AE(nc.sbuf_tensor("QT", [128, NC * NT], BF16))

            with ExitStack() as a2:
                A2 = a2.enter_context
                KTa = A2(nc.sbuf_tensor("KTa", [128, 4 * 2048], BF16))
                Va = A2(nc.sbuf_tensor("Va", [128, 16 * 512], BF16))
                vna = A2(nc.sbuf_tensor("vna", [128, 4 * NT], BF16))

                with ExitStack() as s1p:
                    PS["lnq"] = s1p.enter_context(
                        nc.psum_tensor("lnqps1", [128, 2048], BF16))
                    PS["proj"] = s1p.enter_context(
                        nc.psum_tensor("projps1", [128, 1024], F32))
                    PS["wsl"] = s1p.enter_context(
                        nc.sbuf_tensor("wslice1", [128, NC * 512], BF16))
                    for side in (0, 1):
                        PS["lndump"] = (side == 0)
                        run_phase(phase_lnq, x0, 1024 * side, False)
                        PS["lndump"] = False
                        barrier()
                        if KDBG and side == 0:
                            def fdq(g):
                                g.dma_start(out=d_xqt, in_=xqT[:]).then_inc(gd.h, 16)
                                g.wait_ge(gd.h, gd.v + 16)
                            block.gpsimd(fdq)
                            gd.v += 16
                            barrier()
                        if side == 0:
                            run_phase(phase_proj_fm, wqT, 0, 2048,
                                      lambda m, q: QT[:, NT * m + 512 * q:
                                                      NT * m + 512 * q + 512])
                            barrier()
                        run_phase(phase_proj_fm, wkT, 0, 512,
                                  lambda m, q, s=side: KTa[
                                      :, 2048 * m + NT * s + 512 * q:
                                      2048 * m + NT * s + 512 * q + 512])
                        barrier()
                        run_phase(phase_proj_tm, wvT, 0, 512,
                                  lambda t, j, s=side: Va[
                                      :, 512 * (8 * s + t):
                                      512 * (8 * s + t) + 512])
                        barrier()

                if KDBG2:
                    def fda(g):
                        g.dma_start(out=d_a1, in_=QT[:]).then_inc(gd.h, 16)
                        g.dma_start(out=d_a3, in_=Va[:]).then_inc(gd.h, 16)
                        g.wait_ge(gd.h, gd.v + 32)
                    block.gpsimd(fda)
                    gd.v += 32
                    barrier()

                with ExitStack() as vns:
                    VN = vns.enter_context
                    vps = VN(nc.psum_tensor("vps", [128, 1024], BF16))
                    msA = VN(nc.psum_tensor("msA", [1, 512], F32))
                    msB = VN(nc.psum_tensor("msB", [1, 512], F32))
                    bcp = VN(nc.psum_tensor("bcp", [128, 1024], F32))
                    vnb = VN(nc.sbuf_tensor("vnb", [128, 1024], F32))

                    def phase_vn(em):
                        prev_vn = None
                        for g in range(4):
                            def ftr(t_, g=g):
                                for t in range(8):
                                    ins = t_.transpose(
                                        vps[:, 128 * t:128 * t + 128],
                                        Va[:, 512 * t + 128 * g:
                                           512 * t + 128 * g + 128], ident[:])
                                return ins
                            tr = em.do("tensor", ftr, inc="tc",
                                       wait=[prev_vn] if prev_vn else [])
                            cp = em.do("scalar",
                                       lambda s: s.copy(vnb[:], vps[:]),
                                       inc="sc", wait=[tr])
                            sq = em.do("scalar",
                                       lambda s: s.activation(
                                           xnf[:, 0:1024], vnb[:], AF.Square),
                                       inc="sc", wait=[cp])

                            def fms(t_):
                                t_.matmul(msA[0:1, :], ones_f[:, 0:1],
                                          xnf[:, 0:512], start=True, stop=True)
                                return t_.matmul(msB[0:1, :], ones_f[:, 0:1],
                                                 xnf[:, 512:1024], start=True,
                                                 stop=True)
                            ms = em.do("tensor", fms, inc="tc", wait=[sq])

                            def fpre(v):
                                v.tensor_scalar(rowb2[0:1, 0:512], msA[0:1, :],
                                                1.0, 1e-24, ALU.mult, ALU.add)
                                return v.tensor_scalar(rowb2[0:1, 512:1024],
                                                       msB[0:1, :], 1.0, 1e-24,
                                                       ALU.mult, ALU.add)
                            pre = em.do("vector", fpre, inc="vc", wait=[ms])
                            rs0 = em.do(
                                "scalar",
                                lambda s: s.activation(rowb3[0:1, 0:1024],
                                                       rowb2[0:1, 0:1024],
                                                       AF.Sqrt),
                                inc="sc", wait=[pre])
                            rs = em.do(
                                "vector",
                                lambda v: v.reciprocal(rowb0[0:1, 0:1024],
                                                       rowb3[0:1, 0:1024]),
                                inc="vc", wait=[rs0])

                            def fbc(t_):
                                for q in range(2):
                                    ins = t_.matmul(
                                        bcp[:, 512 * q:512 * q + 512],
                                        ones_f[0:1, :],
                                        rowb0[0:1, 512 * q:512 * q + 512],
                                        start=True, stop=True)
                                return ins
                            bc = em.do("tensor", fbc, inc="tc", wait=[rs])
                            prev_vn = em.do(
                                "vector",
                                lambda v, g=g: v.tensor_tensor(
                                    vna[:, NT * g:NT * g + NT], vnb[:], bcp[:],
                                    ALU.mult),
                                inc="vc", wait=[bc], fence=True)
                    run_phase(phase_vn)
                    barrier()

                with ExitStack() as hctx:
                    HE = hctx.enter_context
                    sps = HE(nc.psum_tensor("sps", [128, 1024], F32))
                    yps = HE(nc.psum_tensor("yps", [128, 1024], F32))
                    dpsA = HE(nc.psum_tensor("dpsA", [1, 512], F32))
                    dpsB = HE(nc.psum_tensor("dpsB", [1, 512], F32))
                    bcps = HE(nc.psum_tensor("bcps", [128, 1024], F32))
                    pT = HE(nc.sbuf_tensor("pT", [128, 2 * 1024], BF16))
                    yf = HE(nc.sbuf_tensor("yf", [128, 1024], F32))
                    tf = HE(nc.sbuf_tensor("tf", [128, 1024], F32))
                    rbf = HE(nc.sbuf_tensor("rbf", [128, 1024], F32))

                    def phase_qknorm(em):
                        prev = None
                        items = [("q", h) for h in range(16)] + \
                                [("k", i) for i in range(8)]
                        for kind, i in items:
                            if kind == "q":
                                src = QT[:, NT * i:NT * i + NT]
                            else:
                                g, half = i // 2, i % 2
                                src = KTa[:, 2048 * g + 1024 * half:
                                          2048 * g + 1024 * half + 1024]
                            sq = em.do("scalar",
                                       lambda s, src=src: s.activation(
                                           xnf[:, 0:1024], src, AF.Square),
                                       inc="sc",
                                       wait=[prev] if prev else [])

                            def fms(t_):
                                t_.matmul(dpsA[0:1, :], ones_f[:, 0:1],
                                          xnf[:, 0:512], start=True, stop=True)
                                return t_.matmul(dpsB[0:1, :], ones_f[:, 0:1],
                                                 xnf[:, 512:1024], start=True,
                                                 stop=True)
                            ms = em.do("tensor", fms, inc="tc", wait=[sq])

                            def fpre(v):
                                v.tensor_scalar(rowb2[0:1, 0:512],
                                                dpsA[0:1, :], 1.0 / 128.0,
                                                RMS_EPS, ALU.mult, ALU.add)
                                return v.tensor_scalar(rowb2[0:1, 512:1024],
                                                       dpsB[0:1, :],
                                                       1.0 / 128.0, RMS_EPS,
                                                       ALU.mult, ALU.add)
                            pre = em.do("vector", fpre, inc="vc", wait=[ms])
                            rs0 = em.do(
                                "scalar",
                                lambda s: s.activation(rowb3[0:1, 0:1024],
                                                       rowb2[0:1, 0:1024],
                                                       AF.Sqrt),
                                inc="sc", wait=[pre])
                            rs = em.do(
                                "vector",
                                lambda v: v.reciprocal(rowb0[0:1, 0:1024],
                                                       rowb3[0:1, 0:1024]),
                                inc="vc", wait=[rs0])

                            def fbc(t_):
                                for q in range(2):
                                    ins = t_.matmul(
                                        bcps[:, 512 * q:512 * q + 512],
                                        ones_f[0:1, :],
                                        rowb0[0:1, 512 * q:512 * q + 512],
                                        start=True, stop=True)
                                return ins
                            bc = em.do("tensor", fbc, inc="tc", wait=[rs])
                            prev = em.do(
                                "vector",
                                lambda v, src=src: v.tensor_tensor(
                                    src, src, bcps[:], ALU.mult),
                                inc="vc", wait=[bc], fence=True)
                    run_phase(phase_qknorm)
                    barrier()
                    if KDBG2:
                        def fdb(g):
                            g.dma_start(out=d_a2, in_=KTa[:]).then_inc(gd.h, 16)
                            g.dma_start(out=d_a4, in_=vna[:]).then_inc(gd.h, 16)
                            g.wait_ge(gd.h, gd.v + 32)
                        block.gpsimd(fdb)
                        gd.v += 32
                        barrier()

                    def phase_heads_a(em):
                        exp_t, dpv_t = {}, {}
                        yq_prev = None
                        for h in range(16):
                            g = h // 4
                            for c in range(16):
                                def fs(t_, g=g, h=h, c=c):
                                    for q in range(2):
                                        ins = t_.matmul(
                                            sps[:, 512 * q:512 * q + 512],
                                            KTa[:, 2048 * g + 128 * c:
                                                2048 * g + 128 * c + 128],
                                            QT[:, NT * h + 512 * q:
                                               NT * h + 512 * q + 512],
                                            start=True, stop=True)
                                    return ins
                                ws = [exp_t[(h, c - 1)]] if c >= 1 else \
                                     ([exp_t[(h - 1, 15)]] if h >= 1 else [])
                                smm = em.do("tensor", fs, inc="tc", wait=ws)
                                pslot = pT[:, 1024 * (c % 2):
                                           1024 * (c % 2) + 1024]

                                def fe(s, pslot=pslot):
                                    for q in range(2):
                                        ins = s.activation(
                                            pslot[:, 512 * q:512 * q + 512],
                                            sps[:, 512 * q:512 * q + 512],
                                            AF.Exp, scale=SC_HD)
                                    return ins
                                we = [smm] + ([dpv_t[(h, c - 2)]]
                                              if c >= 2 else [])
                                exp_t[(h, c)] = em.do("scalar", fe, inc="sc",
                                                      wait=we, fence=True)

                                def fdpv(t_, g=g, c=c, pslot=pslot):
                                    t_.matmul(dpsA[0:1, :], ones_b[:, 0:1],
                                              pslot[:, 0:512],
                                              start=(c == 0), stop=(c == 15))
                                    t_.matmul(dpsB[0:1, :], ones_b[:, 0:1],
                                              pslot[:, 512:1024],
                                              start=(c == 0), stop=(c == 15))
                                    for q in range(2):
                                        ins = t_.matmul(
                                            yps[:, 512 * q:512 * q + 512],
                                            Va[:, 512 * c + 128 * g:
                                               512 * c + 128 * g + 128],
                                            pslot[:, 512 * q:512 * q + 512],
                                            start=(c == 0), stop=(c == 15))
                                    return ins
                                wd = [exp_t[(h, c)]]
                                if c == 0 and yq_prev is not None:
                                    wd.append(yq_prev)
                                dpv_t[(h, c)] = em.do("tensor", fdpv, inc="tc",
                                                      wait=wd)
                            dcp = em.do(
                                "scalar",
                                lambda s: (s.copy(rowb0[0:1, 0:512],
                                                  dpsA[0:1, :]),
                                           s.copy(rowb0[0:1, 512:1024],
                                                  dpsB[0:1, :]))[-1],
                                inc="sc", wait=[dpv_t[(h, 15)]])
                            rcp = em.do(
                                "vector",
                                lambda v: v.reciprocal(rowb1[0:1, 0:1024],
                                                       rowb0[0:1, 0:1024]),
                                inc="vc", wait=[dcp], fence=True)

                            def fbc(t_):
                                for q in range(2):
                                    ins = t_.matmul(
                                        bcps[:, 512 * q:512 * q + 512],
                                        ones_f[0:1, :],
                                        rowb1[0:1, 512 * q:512 * q + 512],
                                        start=True, stop=True)
                                return ins
                            bc = em.do("tensor", fbc, inc="tc", wait=[rcp])
                            ev = em.do("scalar",
                                       lambda s: s.copy(rbf[:], bcps[:]),
                                       inc="sc", wait=[bc])
                            ynorm = em.do(
                                "vector",
                                lambda v: v.tensor_tensor(yf[:], yps[:],
                                                          rbf[:], ALU.mult),
                                inc="vc", wait=[ev, dpv_t[(h, 15)]])
                            tmul = em.do(
                                "vector",
                                lambda v, g=g: v.tensor_tensor(
                                    tf[:], yf[:], vna[:, NT * g:NT * g + NT],
                                    ALU.mult),
                                inc="vc", wait=[ynorm], fence=True)

                            def fcs(t_):
                                t_.matmul(dpsA[0:1, :], ones_f[:, 0:1],
                                          tf[:, 0:512], start=True, stop=True)
                                return t_.matmul(dpsB[0:1, :], ones_f[:, 0:1],
                                                 tf[:, 512:1024], start=True,
                                                 stop=True)
                            cs = em.do("tensor", fcs, inc="tc",
                                       wait=[tmul, dcp])
                            ccp = em.do(
                                "scalar",
                                lambda s: (s.copy(rowb2[0:1, 0:512],
                                                  dpsA[0:1, :]),
                                           s.copy(rowb2[0:1, 512:1024],
                                                  dpsB[0:1, :]))[-1],
                                inc="sc", wait=[cs])

                            def fbc2(t_):
                                for q in range(2):
                                    ins = t_.matmul(
                                        bcps[:, 512 * q:512 * q + 512],
                                        ones_f[0:1, :],
                                        rowb2[0:1, 512 * q:512 * q + 512],
                                        start=True, stop=True)
                                return ins
                            bc2 = em.do("tensor", fbc2, inc="tc",
                                        wait=[ccp, ynorm])
                            ev2 = em.do("scalar",
                                        lambda s: s.copy(rbf[:], bcps[:]),
                                        inc="sc", wait=[bc2])

                            def ford(v, g=g, h=h):
                                v.tensor_tensor(tf[:], rbf[:],
                                                vna[:, NT * g:NT * g + NT],
                                                ALU.mult)
                                return v.tensor_tensor(
                                    QT[:, NT * h:NT * h + NT], yf[:], tf[:],
                                    ALU.subtract)
                            yq_prev = em.do("vector", ford, inc="vc",
                                            wait=[ev2])
                    run_phase(phase_heads_a)
                    barrier()
                    if KDBG2:
                        def fdc(g):
                            g.dma_start(out=d_a5, in_=QT[:]).then_inc(gd.h, 16)
                            g.wait_ge(gd.h, gd.v + 16)
                        block.gpsimd(fdc)
                        gd.v += 16
                        barrier()

            with ExitStack() as wctx:
                PS["wfull"] = wctx.enter_context(
                    nc.sbuf_tensor("wfullA", [128, NC * 2048], BF16))
                PS["proj"] = wctx.enter_context(
                    nc.psum_tensor("projpsA", [128, 1024], F32))
                run_phase(phase_out_proj, woT, QT, x0, x_hbm, "gd")
                barrier()
                if KDBG:
                    dbg_copy(d_x[0], x_hbm)
                    dbg_copy(d_wq, wqT)
                    dbg_copy(d_wg, wgT)
                    dbg_copy(d_wd, wdT)
                    dbg_copy(d_wu, wuT)

        owidths = []
        _o = 0
        while _o < DFF:
            ow = min(512, DFF - _o)
            owidths.append((_o, ow))
            _o += ow

        def phase_ffn_block(em, blk):
            gups, trps, amps, bc512 = (PS["gups"], PS["trps"], PS["amps"],
                                       PS["bc512"])
            hT, gsl, usl, sgf, hidb, qsc = (PS["hT"], PS["gsl"], PS["usl"],
                                            PS["sgf"], PS["hidb"], PS["qsc"])
            ts = [4 * blk + i for i in range(4)]
            am_t, tr_done, hid_t = {}, {}, {}
            z = em.do("vector", lambda v: v.memset(stats[:, 16:20], 0.0),
                      inc="vc")
            prev_tr = None
            for ji, (oo, ow) in enumerate(owidths):
                wld = ([tr_done[(ji - 1, tt)] for tt in range(4)]
                       if ji >= 1 else [])
                tok_g = em.do(
                    "sync",
                    lambda sy, oo=oo, ow=ow: sy.dma_start(
                        out=gsl[:, 0:NC * ow].rearrange("p (k o) -> p k o",
                                                        k=NC),
                        in_=wgT[:, oo:oo + ow].rearrange("(k p) o -> p k o",
                                                         p=128)),
                    inc=("ld", 16), wait=wld)
                tok_u = em.do(
                    "sync",
                    lambda sy, oo=oo, ow=ow: sy.dma_start(
                        out=usl[:, 0:NC * ow].rearrange("p (k o) -> p k o",
                                                        k=NC),
                        in_=wuT[:, oo:oo + ow].rearrange("(k p) o -> p k o",
                                                         p=128)),
                    inc=("ld", 16), wait=wld)

                for ti, t in enumerate(ts):
                    def fg(t_, t=t, ow=ow):
                        for c in range(NC):
                            ins = t_.matmul(
                                gups[:, 0:ow],
                                xqT[:, NT * c + 128 * t:NT * c + 128 * t + 128],
                                gsl[:, ow * c:ow * c + ow],
                                start=(c == 0), stop=(c == NC - 1))
                        return ins
                    wg_ = [tok_g]
                    hid_prev = hid_t.get((ji, ti - 1)) or \
                        (hid_t.get((ji - 1, 3)) if ji >= 1 else None)
                    if hid_prev is not None:
                        wg_.append(hid_prev)
                    gmm = em.do("tensor", fg, inc="tc", wait=wg_)

                    def fu(t_, t=t, ow=ow):
                        for c in range(NC):
                            ins = t_.matmul(
                                gups[:, 512:512 + ow],
                                xqT[:, NT * c + 128 * t:NT * c + 128 * t + 128],
                                usl[:, ow * c:ow * c + ow],
                                start=(c == 0), stop=(c == NC - 1))
                        return ins
                    umm = em.do("tensor", fu, inc="tc", wait=[tok_u])
                    sil = em.do(
                        "scalar",
                        lambda s, ow=ow: s.activation(sgf[:, 0:ow],
                                                      gups[:, 0:ow], AF.Silu),
                        inc="sc", wait=[gmm])
                    hsl = hidb[:, 512 * (ti % 2):512 * (ti % 2) + 512]
                    whid = [sil, umm]
                    hprev = (ji, ti - 2) if ti >= 2 else (ji - 1, ti + 2)
                    if hprev in tr_done:
                        whid.append(tr_done[hprev])
                    if hprev in am_t:
                        whid.append(am_t[hprev])
                    hid = em.do(
                        "vector",
                        lambda v, hsl=hsl, ow=ow: v.tensor_tensor(
                            hsl[:, 0:ow], sgf[:, 0:ow], gups[:, 512:512 + ow],
                            ALU.mult),
                        inc="vc", wait=whid, fence=True)
                    hid_t[(ji, ti)] = hid
                    nkk = (ow + 127) // 128

                    def ftr(t_, hsl=hsl, nkk=nkk):
                        for kk in range(nkk):
                            ins = t_.transpose(
                                trps[:, 128 * kk:128 * kk + 128],
                                hsl[:, 128 * kk:128 * kk + 128], ident[:])
                        return ins
                    wtr = [hid] + ([prev_tr] if prev_tr else [])
                    trk = em.do("tensor", ftr, inc="tc", wait=wtr)

                    def fev(s, oo=oo, ow=ow, ti=ti, nkk=nkk):
                        ic0 = oo // 128
                        for kk in range(nkk):
                            ins = s.copy(
                                hT[:, 512 * (ic0 + kk) + 128 * ti:
                                   512 * (ic0 + kk) + 128 * ti + 128],
                                trps[:, 128 * kk:128 * kk + 128])
                        return ins
                    evt = em.do("scalar", fev, inc="sc", wait=[trk])
                    prev_tr = evt
                    tr_done[(ji, ti)] = evt

                    def fam(v, hsl=hsl, ti=ti, ow=ow):
                        v.reduce_max(stats[:, 20:21], hsl[:, 0:ow], axis=AX.X,
                                     apply_absolute_value=True)
                        v.drain()
                        v.drain()
                        return v.tensor_scalar(stats[:, 16 + ti:17 + ti],
                                               stats[:, 20:21],
                                               stats[:, 16 + ti:17 + ti], 0.0,
                                               ALU.max, ALU.add)
                    am_t[(ji, ti)] = em.do(
                        "vector", fam, inc="vc", wait=[hid, z, evt],
                        fence=(ji == len(owidths) - 1))
            amrow = PS["amrow"]
            amT = em.do(
                "tensor",
                lambda t_: t_.transpose(amps[0:4, 0:128], stats[:, 16:20],
                                        identf[:]),
                inc="tc",
                wait=[am_t[(len(owidths) - 1, tt)] for tt in range(4)] +
                     [tr_done[(len(owidths) - 1, tt)] for tt in range(4)])
            amcp = em.do("scalar",
                         lambda s: s.copy(amrow[:], amps[0:4, 0:128]),
                         inc="sc", wait=[amT])

            def fmv(g):
                for tt in range(4):
                    g.dma_start(out=rowb0[0:1, 128 * tt:128 * tt + 128],
                                in_=amrow[tt:tt + 1, :]).then_inc(gd.h, 16)
                return None
            amv = em.do("gpsimd", fmv, inc=("gd", 64), wait=[amcp],
                        manual=True)

            def frow(v):
                dm = lambda: v.drain()
                v.tensor_scalar(rowb0[0:1, 0:512], rowb0[0:1, 0:512], Q_EPS,
                                0.0, ALU.max, ALU.add)
                dm()
                dm()
                v.reciprocal(rowb1[0:1, 0:512], rowb0[0:1, 0:512])
                v.tensor_scalar(rowb3[0:1, 0:512], rowb0[0:1, 0:512],
                                1.0 / 127.0, 0.0, ALU.mult, ALU.add)
                dm()
                return v.tensor_scalar(rowb2[0:1, 0:512], rowb1[0:1, 0:512],
                                       127.0, 0.0, ALU.mult, ALU.add)
            rows = em.do("vector", frow, inc="vc", wait=[amv], fence=True)

            def fbc(t_):
                for tt in range(4):
                    t_.matmul(bc512[:, 128 * tt:128 * tt + 128],
                              ones_f[0:1, :],
                              rowb2[0:1, 128 * tt:128 * tt + 128],
                              start=True, stop=True)
                for tt in range(4):
                    ins = t_.matmul(bc512[:, 512 + 128 * tt:
                                          512 + 128 * tt + 128],
                                    ones_f[0:1, :],
                                    rowb3[0:1, 128 * tt:128 * tt + 128],
                                    start=True, stop=True)
                return ins
            bcq = em.do("tensor", fbc, inc="tc", wait=[rows])
            evq = em.do("scalar", lambda s: s.copy(qsc[:], bc512[:]),
                        inc="sc", wait=[bcq])
            skipq = int(os.environ.get("KFQ0", "0"))
            for ic in range(NIC):
                def fq(v, ic=ic):
                    if skipq:
                        return v.tensor_copy(xnf[:, 0:512],
                                             hT[:, 512 * ic:512 * ic + 512])
                    v.tensor_tensor(xnf[:, 0:512],
                                    hT[:, 512 * ic:512 * ic + 512],
                                    qsc[:, 0:512], ALU.mult)
                    v.tensor_scalar(xnf[:, 0:512], xnf[:, 0:512], CMAG, -CMAG,
                                    ALU.add, ALU.add)
                    return v.tensor_tensor(hT[:, 512 * ic:512 * ic + 512],
                                           xnf[:, 0:512], qsc[:, 512:1024],
                                           ALU.mult)
                em.do("vector", fq, inc="vc", wait=[evq])

        def phase_ffn_down(em, blk, resid_src, resid_dst, store_ctr):
            gups = PS["gups"]
            hT, xsl2 = PS["hT"], PS["xsl2"]
            wdsl = PS["wbig"]
            ts = [4 * blk + i for i in range(4)]
            mm_t, add_t, st_t = {}, {}, {}
            for jj in range(4):
                wld = ([mm_t[(jj - 1, tt)] for tt in range(4)]
                       if jj >= 1 else [])
                tok_w = em.do(
                    "sync",
                    lambda sy, jj=jj: sy.dma_start(
                        out=wdsl[:, 0:NIC * 512].rearrange(
                            "p (k o) -> p k o", k=NIC),
                        in_=wdT[:, 512 * jj:512 * jj + 512].rearrange(
                            "(k p) o -> p k o", p=128)),
                    inc=("ld", 16), wait=wld)
                for ti, t in enumerate(ts):
                    xs = xsl2[:, 512 * ti:512 * ti + 512]
                    wlx = [st_t[(jj - 1, ti)]] if jj >= 1 else []
                    ldx = em.do(
                        "sync",
                        lambda sy, t=t, jj=jj, xs=xs: sy.dma_start(
                            out=xs,
                            in_=resid_src[128 * t:128 * t + 128,
                                          512 * jj:512 * jj + 512]),
                        inc=("ld", 16), wait=wlx)

                    def fmm(t_, ti=ti):
                        for ic in range(NIC):
                            ins = t_.matmul(
                                gups[:, 512 * (ti % 2):512 * (ti % 2) + 512],
                                hT[:, 512 * ic + 128 * ti:
                                   512 * ic + 128 * ti + 128],
                                wdsl[:, 512 * ic:512 * ic + 512],
                                start=(ic == 0), stop=(ic == NIC - 1))
                        return ins
                    wmm = [tok_w]
                    prev = (jj, ti - 2) if ti >= 2 else (jj - 1, ti + 2)
                    if prev in add_t:
                        wmm.append(add_t[prev])
                    mm_t[(jj, ti)] = em.do("tensor", fmm, inc="tc", wait=wmm)
                    add_t[(jj, ti)] = em.do(
                        "vector",
                        lambda v, ti=ti, xs=xs: v.tensor_tensor(
                            xs, gups[:, 512 * (ti % 2):512 * (ti % 2) + 512],
                            xs, ALU.add),
                        inc="vc", wait=[mm_t[(jj, ti)], ldx], fence=True)
                    st_t[(jj, ti)] = em.do(
                        "gpsimd",
                        lambda g, t=t, jj=jj, xs=xs: g.dma_start(
                            out=resid_dst[128 * t:128 * t + 128,
                                          512 * jj:512 * jj + 512], in_=xs),
                        inc=(store_ctr, 16), wait=[add_t[(jj, ti)]])

        def emit_ffn(resid_src, resid_dst, store_ctr, tag):
            with ExitStack() as fctx:
                FE = fctx.enter_context
                PS["lnq"] = FE(nc.psum_tensor("lnqpsF" + tag, [128, 2048], BF16))
                PS["hT"] = FE(nc.sbuf_tensor("hT" + tag, [128, NIC * 512], BF16))
                wbig = FE(nc.sbuf_tensor("wbig" + tag, [128, NIC * 512], BF16))
                PS["wbig"] = wbig
                PS["gsl"] = wbig[:, 0:NC * 512]
                PS["usl"] = wbig[:, NC * 512:2 * NC * 512]
                PS["sgf"] = FE(nc.sbuf_tensor("sgf" + tag, [128, 512], F32))
                PS["hidb"] = FE(nc.sbuf_tensor("hidb" + tag, [128, 2 * 512], BF16))
                PS["qsc"] = FE(nc.sbuf_tensor("qsc" + tag, [128, 2 * 512], F32))
                PS["xsl2"] = FE(nc.sbuf_tensor("xsl2" + tag, [128, 4 * 512], F32))
                PS["amrow"] = FE(nc.sbuf_tensor("amrow" + tag, [4, 128], F32))
                run_phase(phase_lnq, resid_src, 0, True)
                barrier()
                if KDBG2 and tag == "F2":
                    def fdf1(g):
                        g.dma_start(out=d_f1, in_=xqT[:]).then_inc(gd.h, 16)
                        g.wait_ge(gd.h, gd.v + 16)
                    block.gpsimd(fdf1)
                    gd.v += 16
                    barrier()
                for blk in range(2):
                    with ExitStack() as bps:
                        sfx = tag + str(blk)
                        PS["gups"] = bps.enter_context(
                            nc.psum_tensor("gups" + sfx, [128, 1024], F32))
                        PS["trps"] = bps.enter_context(
                            nc.psum_tensor("trps" + sfx, [128, 512], BF16))
                        PS["amps"] = bps.enter_context(
                            nc.psum_tensor("amps" + sfx, [128, 128], F32))
                        PS["bc512"] = bps.enter_context(
                            nc.psum_tensor("bc512" + sfx, [128, 1024], F32))
                        run_phase(phase_ffn_block, blk)
                        barrier()
                        if KDBG2 and tag == "F2" and blk == 0:
                            def fdf2(g):
                                g.dma_start(out=d_f2, in_=PS["hT"][:]
                                            ).then_inc(gd.h, 16)
                                g.dma_start(out=d_f3, in_=PS["qsc"][:]
                                            ).then_inc(gd.h, 16)
                                g.dma_start(out=d_f4[0:128, :],
                                            in_=stats[:, 16:32]
                                            ).then_inc(gd.h, 16)
                                g.dma_start(out=d_f4[128:132, 0:16],
                                            in_=PS["amrow"][:, 0:16]
                                            ).then_inc(gd.h, 16)
                                g.dma_start(out=d_f4[256:257, 0:16],
                                            in_=rowb0[0:1, 0:16]
                                            ).then_inc(gd.h, 16)
                                g.wait_ge(gd.h, gd.v + 80)
                            block.gpsimd(fdf2)
                            gd.v += 80
                            barrier()
                        run_phase(phase_ffn_down, blk, resid_src, resid_dst,
                                  store_ctr)
                        barrier()

        emit_ffn(x_hbm, x_hbm, "gd", "F2")

        if KDBG:
            dbg_copy(d_x[1], x_hbm)

        with ExitStack() as bctx:
            BA = bctx.enter_context
            QT = BA(nc.sbuf_tensor("QTb", [128, NC * NT], BF16))

            with ExitStack() as b2:
                B2 = b2.enter_context
                KTb = B2(nc.sbuf_tensor("KTb", [128, NC * NT], BF16))
                Vb = B2(nc.sbuf_tensor("Vb", [128, 8 * 2048], BF16))
                with ExitStack() as b3:
                    PS["lnq"] = b3.enter_context(
                        nc.psum_tensor("lnqpsB", [128, 2048], BF16))
                    PS["proj"] = b3.enter_context(
                        nc.psum_tensor("projpsB", [128, 1024], F32))
                    PS["wsl"] = b3.enter_context(
                        nc.sbuf_tensor("wslice3", [128, NC * 512], BF16))
                    run_phase(phase_lnq, x_hbm, 0, True)
                    barrier()
                    run_phase(phase_proj_fm, wqkvT, 0, 2048,
                              lambda m, q: QT[:, NT * m + 512 * q:
                                              NT * m + 512 * q + 512])
                    barrier()
                    run_phase(phase_proj_fm, wqkvT, 2048, 4096,
                              lambda m, q: KTb[:, NT * m + 512 * q:
                                               NT * m + 512 * q + 512])
                    barrier()
                    run_phase(phase_proj_tm, wqkvT, 4096, 6144,
                              lambda t, j: Vb[:, 2048 * t + 512 * j:
                                              2048 * t + 512 * j + 512])
                    barrier()

                GK0 = gd.v

                def p_kv_up(g):
                    pid = g.partition_id()
                    for par in (0, 1):
                        cm = g.If((pid % 2) == 0) if par == 0 else g.Else()
                        with cm:
                            g.dma_start(
                                out=kT_sh[par].rearrange("(k p) n -> p k n",
                                                         p=128),
                                in_=KTb[:].rearrange("p (k n) -> p k n", k=NC)
                            ).then_inc(gd.h, 16)
                            g.dma_start(
                                out=v_sh[par].rearrange("(t p) d -> p t d",
                                                        p=128),
                                in_=Vb[:].rearrange("p (t d) -> p t d", t=8)
                            ).then_inc(gd.h, 16)
                            g.wait_ge(gd.h, GK0 + 32)
                            g.dma_start(out=flag_sh[par, 1],
                                        in_=snonce[0:1, 8:16]
                                        ).then_inc(gd.h, 16)
                            g.wait_ge(gd.h, GK0 + 48)
                            emit_poll(g, par, 1)
                block.gpsimd(p_kv_up)
                gd.v += 48
                barrier()

            with ExitStack() as hctx2:
                HB = hctx2.enter_context
                sps = HB(nc.psum_tensor("spsB", [128, 1024], F32))
                yps = HB(nc.psum_tensor("ypsB", [128, 1024], F32))
                dpsA = HB(nc.psum_tensor("dpsAB", [1, 512], F32))
                dpsB = HB(nc.psum_tensor("dpsBB", [1, 512], F32))
                bcps = HB(nc.psum_tensor("bcpsB", [128, 1024], F32))
                pT = HB(nc.sbuf_tensor("pTB", [128, 2 * 1024], BF16))
                yf = HB(nc.sbuf_tensor("yfB", [128, 1024], F32))
                rbf = HB(nc.sbuf_tensor("rbfB", [128, 1024], F32))
                maskb = HB(nc.sbuf_tensor("maskb", [128, 16 * NT], BF16))
                khb = HB(nc.sbuf_tensor("khb", [128, 2 * 2048], BF16))
                vhb = HB(nc.sbuf_tensor("vhb", [128, 2 * 2048], BF16))

                def phase_heads_b(em):
                    mload = em.do(
                        "sync",
                        lambda sy: sy.dma_start(out=maskb[:], in_=maskin),
                        inc=("ld", 16))
                    kl_t, vl_t, exp_t, msk_t, dpv_t = {}, {}, {}, {}, {}
                    yq_prev = None
                    for h in range(16):
                        ksl = khb[:, 2048 * (h % 2):2048 * (h % 2) + 2048]
                        vsl = vhb[:, 2048 * (h % 2):2048 * (h % 2) + 2048]
                        wkv = [dpv_t[(h - 2, 15)]] if h >= 2 else []

                        def fkl(sy, h=h, ksl=ksl):
                            for s2 in range(2):
                                sy.dma_start(
                                    out=ksl[:, 1024 * s2:1024 * s2 + 1024],
                                    in_=kT_sh[s2, 128 * h:128 * h + 128, :]
                                ).then_inc(ld.h, 16)
                            return None
                        kl_t[h] = em.do("sync", fkl, inc=("ld", 32), wait=wkv,
                                        manual=True)

                        def fvl(sy, h=h, vsl=vsl):
                            for i in range(16):
                                sy.dma_start(
                                    out=vsl[:, 128 * i:128 * i + 128],
                                    in_=v_sh[i // 8,
                                             128 * (i % 8):128 * (i % 8) + 128,
                                             128 * h:128 * h + 128]
                                ).then_inc(ld.h, 16)
                            return None
                        vl_t[h] = em.do("sync", fvl, inc=("ld", 256), wait=wkv,
                                        manual=True)
                        for c in range(16):
                            def fs(t_, h=h, c=c, ksl=ksl):
                                for q in range(2):
                                    ins = t_.matmul(
                                        sps[:, 512 * q:512 * q + 512],
                                        ksl[:, 128 * c:128 * c + 128],
                                        QT[:, NT * h + 512 * q:
                                           NT * h + 512 * q + 512],
                                        start=True, stop=True)
                                return ins
                            ws = [kl_t[h]]
                            if c >= 1:
                                ws.append(exp_t[(h, c - 1)])
                            elif h >= 1:
                                ws.append(exp_t[(h - 1, 15)])
                            smm = em.do("tensor", fs, inc="tc", wait=ws)
                            pslot = pT[:, 1024 * (c % 2):1024 * (c % 2) + 1024]
                            wm = [mload, smm]
                            msk_t[(h, c)] = em.do(
                                "vector",
                                lambda v, c=c: v.tensor_tensor(
                                    sps[:], sps[:],
                                    maskb[:, NT * c:NT * c + NT], ALU.add),
                                inc="gc", wait=wm, fence=True)

                            def fe(s, pslot=pslot):
                                for q in range(2):
                                    ins = s.activation(
                                        pslot[:, 512 * q:512 * q + 512],
                                        sps[:, 512 * q:512 * q + 512],
                                        AF.Exp, scale=SC_HD)
                                return ins
                            we = [msk_t[(h, c)]] + \
                                 ([dpv_t[(h, c - 2)]] if c >= 2 else [])
                            exp_t[(h, c)] = em.do("scalar", fe, inc="sc",
                                                  wait=we, fence=True)

                            def fdpv(t_, c=c, pslot=pslot, vsl=vsl):
                                t_.matmul(dpsA[0:1, :], ones_b[:, 0:1],
                                          pslot[:, 0:512], start=(c == 0),
                                          stop=(c == 15))
                                t_.matmul(dpsB[0:1, :], ones_b[:, 0:1],
                                          pslot[:, 512:1024], start=(c == 0),
                                          stop=(c == 15))
                                for q in range(2):
                                    ins = t_.matmul(
                                        yps[:, 512 * q:512 * q + 512],
                                        vsl[:, 128 * c:128 * c + 128],
                                        pslot[:, 512 * q:512 * q + 512],
                                        start=(c == 0), stop=(c == 15))
                                return ins
                            wd = [exp_t[(h, c)], vl_t[h]]
                            if c == 0 and yq_prev is not None:
                                wd.append(yq_prev)
                            dpv_t[(h, c)] = em.do("tensor", fdpv, inc="tc",
                                                  wait=wd)
                        dcp = em.do(
                            "scalar",
                            lambda s: (s.copy(rowb0[0:1, 0:512], dpsA[0:1, :]),
                                       s.copy(rowb0[0:1, 512:1024],
                                              dpsB[0:1, :]))[-1],
                            inc="sc", wait=[dpv_t[(h, 15)]])
                        rcp = em.do(
                            "vector",
                            lambda v: v.reciprocal(rowb1[0:1, 0:1024],
                                                   rowb0[0:1, 0:1024]),
                            inc="vc", wait=[dcp])

                        def fbc(t_):
                            for q in range(2):
                                ins = t_.matmul(
                                    bcps[:, 512 * q:512 * q + 512],
                                    ones_f[0:1, :],
                                    rowb1[0:1, 512 * q:512 * q + 512],
                                    start=True, stop=True)
                            return ins
                        bc = em.do("tensor", fbc, inc="tc", wait=[rcp])
                        ev = em.do("scalar", lambda s: s.copy(rbf[:], bcps[:]),
                                   inc="sc", wait=[bc])
                        yq_prev = em.do(
                            "vector",
                            lambda v, h=h: v.tensor_tensor(
                                QT[:, NT * h:NT * h + NT], yps[:], rbf[:],
                                ALU.mult),
                            inc="vc", wait=[ev, dpv_t[(h, 15)]])
                run_phase(phase_heads_b)
                barrier()

            with ExitStack() as yq:
                PS["lnq"] = yq.enter_context(
                    nc.psum_tensor("lnqpsY", [128, 2048], BF16))

                def phase_yquant(em):
                    prev_fin = None
                    for t in range(NTILE):
                        def ftr(t_, t=t):
                            for c in range(NC):
                                ins = t_.transpose(
                                    PS["lnq"][:, 128 * c:128 * c + 128],
                                    QT[:, NT * c + 128 * t:
                                       NT * c + 128 * t + 128], ident[:])
                            return ins
                        tr = em.do("tensor", ftr, inc="tc",
                                   wait=[prev_fin] if prev_fin else [])
                        ev = em.do(
                            "scalar",
                            lambda s, t=t: s.copy(
                                xqtok[:, D * (t % 2):D * (t % 2) + D],
                                PS["lnq"][:]),
                            inc="sc", wait=[tr])

                        def fq(v, t=t):
                            dm = lambda: v.drain()
                            qsl = xqtok[:, D * (t % 2):D * (t % 2) + D]
                            v.reduce_max(stats[:, 9:10], qsl, axis=AX.X,
                                         apply_absolute_value=True)
                            dm()
                            dm()
                            v.tensor_scalar(stats[:, 10:11], stats[:, 9:10],
                                            Q_EPS, 0.0, ALU.max, ALU.add)
                            dm()
                            dm()
                            v.reciprocal(stats[:, 11:12], stats[:, 10:11])
                            v.tensor_scalar(stats[:, 13:14], stats[:, 10:11],
                                            1.0 / 127.0, 0.0, ALU.mult,
                                            ALU.add)
                            dm()
                            v.tensor_scalar(stats[:, 12:13], stats[:, 11:12],
                                            127.0, 0.0, ALU.mult, ALU.add)
                            dm()
                            dm()
                            v.tensor_scalar(xnf[:, 0:D], qsl, stats[:, 12:13],
                                            CMAG, ALU.mult, ALU.add)
                            return v.tensor_scalar(qsl, xnf[:, 0:D], -CMAG,
                                                   stats[:, 13:14], ALU.add,
                                                   ALU.mult)
                        qk = em.do("vector", fq, inc="vc", wait=[ev],
                                   fence=True)

                        def ftr2(t_, t=t):
                            qsl = xqtok[:, D * (t % 2):D * (t % 2) + D]
                            for c in range(NC):
                                ins = t_.transpose(
                                    PS["lnq"][:, 128 * c:128 * c + 128],
                                    qsl[:, 128 * c:128 * c + 128], ident[:])
                            return ins
                        tr2 = em.do("tensor", ftr2, inc="tc", wait=[qk])

                        def fev2(s, t=t):
                            for c in range(NC):
                                ins = s.copy(xqT[:, NT * c + 128 * t:
                                                 NT * c + 128 * t + 128],
                                             PS["lnq"][:, 128 * c:128 * c + 128])
                            return ins
                        prev_fin = em.do("scalar", fev2, inc="sc", wait=[tr2])
                run_phase(phase_yquant)
                barrier()

            with ExitStack() as wctx2:
                PS["wfull"] = wctx2.enter_context(
                    nc.sbuf_tensor("wfullB", [128, NC * 2048], BF16))
                PS["proj"] = wctx2.enter_context(
                    nc.psum_tensor("projpsB2", [128, 1024], F32))
                run_phase(phase_out_proj, wprojT, xqT, x_hbm, x_hbm, "gd")
                barrier()
                if KDBG:
                    dbg_copy(d_x[2], x_hbm)

        emit_ffn(x_hbm, out, "st", "F4")

        def p_final(g):
            g.wait_ge(st.h, st.v)
        block.gpsimd(p_final)

    return nc


_BUILT = [None]


def kernel(**inputs):
    x = np.asarray(inputs["L0_attn_a_x"], dtype=np.float32)
    Wq = np.asarray(inputs["L0_attn_a_Wq"], dtype=np.float32)
    Wk = np.asarray(inputs["L0_attn_a_Wk"], dtype=np.float32)
    Wv = np.asarray(inputs["L0_attn_a_Wv"], dtype=np.float32)
    Wo = np.asarray(inputs["L0_attn_a_Wo"], dtype=np.float32)
    Wqkv = np.asarray(inputs["L0_attn_b_w_qkv"], dtype=np.float32)
    Wproj = np.asarray(inputs["L0_attn_b_w_proj"], dtype=np.float32)
    Wg = np.asarray(inputs["L0_ffn_w_gate"], dtype=np.float32)
    Wu = np.asarray(inputs["L0_ffn_w_up"], dtype=np.float32)
    Wd = np.asarray(inputs["L0_ffn_w_down"], dtype=np.float32)

    if _BUILT[0] is None:
        _BUILT[0] = build()
    nc = _BUILT[0]

    ident_b = np.eye(128, dtype=ml_dtypes.bfloat16)
    ident_f = np.eye(128, dtype=np.float32)
    base = np.random.randint(1 << 28, 1 << 30, dtype=np.int32)
    nonce = np.zeros((1, 32), dtype=np.int32)
    for e in range(4):
        nonce[0, 8 * e:8 * e + 8] = base + e

    WprojPad = np.zeros((6144, 2048), dtype=np.float32)
    WprojPad[0:2048, :] = Wproj

    in_maps = []
    for r in range(8):
        b, half = r // 2, r % 2
        own = x[b, 1024 * half:1024 * half + 1024, :]
        other = x[b, 1024 * (1 - half):1024 * (1 - half) + 1024, :]
        x0r = np.concatenate([own, other], axis=0)
        q0 = 1024 * half
        m = np.zeros((128, 16 * NT), dtype=ml_dtypes.bfloat16)
        rpart = np.arange(128)[:, None]
        for c in range(16):
            kabs = 128 * c + rpart
            allow = (q0 + np.arange(NT)[None, :]) >= kabs
            m[:, NT * c:NT * c + NT] = np.where(
                allow, 0.0, -30000.0).astype(ml_dtypes.bfloat16)
        if half == 0:
            im = {"w_a1": Wq, "w_a2": Wk, "w_a3": Wv, "w_g": Wqkv,
                  "w_t1": Wg, "w_t2": np.zeros((2048, 5504), np.float32)}
        else:
            im = {"w_a1": Wo, "w_a2": np.zeros((512, 2048), np.float32),
                  "w_a3": np.zeros((512, 2048), np.float32), "w_g": WprojPad,
                  "w_t1": Wu, "w_t2": Wd}
        im.update({"x0": np.ascontiguousarray(x0r), "maskin": m,
                   "identin": ident_b, "identfin": ident_f,
                   "noncein": nonce})
        in_maps.append(im)

    res = run_bass_kernel_spmd(nc, in_maps, core_ids=list(range(8)))
    outp = np.zeros((4, 2048, D), dtype=np.float32)
    for r in range(8):
        b, half = r // 2, r % 2
        outp[b, 1024 * half:1024 * half + 1024, :] = res.results[r]["out"]
    return outp
